# version 31
# baseline (speedup 1.0000x reference)
import numpy as np
import ml_dtypes
import concourse.bass as bass
import concourse.mybir as mybir
from concourse.bass_utils import run_bass_kernel_spmd
from contextlib import ExitStack

F32 = mybir.dt.float32
BF16 = mybir.dt.bfloat16
I32 = mybir.dt.int32
AF = mybir.ActivationFunctionType
ALU = mybir.AluOpType
BF = ml_dtypes.bfloat16

D = 4096
NCH = 32
TH = 512
G = 8
KQ = 8
THR = 0.05
EPS = 1e-5
LN256 = float(np.log(256.0))


class Ctr:
    __slots__ = ("sem", "n", "name")

    def __init__(self, sem, name):
        self.sem = sem
        self.n = 0
        self.name = name


def build_program(parts=6, dump=False):
    nc = bass.Bass("TRN2", target_bir_lowering=False,
                   detect_race_conditions=False)

    xT = nc.dram_tensor("xT", [D, 2 * TH], F32, kind="ExternalInput")
    maskd = nc.dram_tensor("maskd", [D, D], I32, kind="ExternalInput")
    waT = nc.dram_tensor("waT", [D, D], F32, kind="ExternalInput")
    wf = nc.dram_tensor("wf", [D, D], F32, kind="ExternalInput")
    bias_r = nc.dram_tensor("bias_r", [128, 32], F32, kind="ExternalInput")
    RepN_d = nc.dram_tensor("RepN", [128, D], BF16, kind="ExternalInput")
    RepT_d = nc.dram_tensor("RepT", [128, 256], BF16, kind="ExternalInput")
    ident_d = nc.dram_tensor("ident", [128, 128], BF16, kind="ExternalInput")
    onesK_d = nc.dram_tensor("onesK", [128, 1], BF16, kind="ExternalInput")
    ones1f_d = nc.dram_tensor("ones1f", [1, 128], F32, kind="ExternalInput")
    outT = nc.dram_tensor("outT", [D, 2 * TH], F32, kind="ExternalOutput")
    dbg_bm = nc.dram_tensor("dbg_bm", [128, 128], BF16, kind="ExternalOutput") if dump else None
    dbg_zt = nc.dram_tensor("dbg_zt", [128, D], BF16, kind="ExternalOutput") if dump else None
    dbg_rw = nc.dram_tensor("dbg_rw", [128, D], BF16, kind="ExternalOutput") if dump else None
    wt2c = nc.dram_tensor("wt2c", [64, 128, 2048], BF16, kind="Internal")
    wmc = nc.dram_tensor("wmc", [64, 128, 2048], BF16, kind="Internal")

    units_all = [(0, "f1"), (0, "at"), (0, "f3"), (1, "f1"), (1, "at"), (1, "f3")]
    units = units_all[:parts]
    have_attn = any(k == "at" for _, k in units)

    plan = []
    last_wait = {}

    def E(engine, emitfn, waits=(), inc=None, k=1):
        w = []
        for (c, v) in waits:
            assert v is not None, "None wait value"
            if v <= 0:
                continue
            assert v <= c.n, f"wait on {c.name} for {v} but only {c.n} emitted"
            key = (engine, id(c))
            if last_wait.get(key, 0) >= v:
                continue
            last_wait[key] = v
            w.append((c.sem, v))
        tgt = None
        if inc is not None:
            inc.n += k
            tgt = (inc.sem, k)

        def f(eng, w=w, emitfn=emitfn, tgt=tgt):
            for s, v in w:
                eng.wait_ge(s, v)
            inst = emitfn(eng)
            if tgt is not None and inst is not None:
                inst.then_inc(*tgt)
        plan.append((engine, f))
        return inc.n if inc is not None else None

    with ExitStack() as ctx:
        def sb(name, shape, dtype):
            return ctx.enter_context(nc.sbuf_tensor(name, shape, dtype))

        def psb(name):
            return ctx.enter_context(nc.psum_tensor(name, [128, 512], F32))

        res = sb("res", [128, NCH * 512], F32)
        lnT = sb("lnT", [128, NCH * 512], BF16)
        wstage = [sb(f"wstage{i}", [128, 2048], F32) for i in range(2)]
        wtile = [sb(f"wtile{i}", [128, 2048], BF16) for i in range(3)]
        wcast = [sb(f"wcast{i}", [128, 2048], BF16) for i in range(3)]
        mtile = [sb(f"mtile{i}", [128, 1024], BF16) for i in range(4)]
        btern = sb("btern", [128, 1024], BF16)
        xb = [sb(f"xb{i}", [128, 512], BF16) for i in range(4)]
        sq = [sb(f"sq{i}", [128, 512], BF16) for i in range(4)]
        t1b = [sb(f"t1b{i}", [128, 512], F32) for i in range(2)]
        tv = [sb(f"tv{i}", [128, 512], F32) for i in range(2)]
        i0b = sb("i0b", [128, 512], I32)
        r0b = [sb(f"r0b{i}", [128, 512], F32) for i in range(2)]
        d2b = [sb(f"d2b{i}", [128, 512], F32) for i in range(2)]
        b1b = sb("b1b0", [128, 512], F32)
        tqb = sb("tqb0", [128, 512], F32)
        qqb = sb("qqb0", [128, 512], F32)
        Rb = sb("Rb", [128, 512], F32)
        MRb = sb("MRb", [128, 512], F32)
        B1 = sb("B1", [1, 512], F32)
        B2 = sb("B2", [1, 512], F32)
        B3 = sb("B3", [1, 512], F32)
        zrow = sb("zrow", [128, D], BF16)
        r32T = [sb(f"r32T{i}", [128, 128], BF16) for i in range(2)]
        bmT = sb("bmT", [128, 128], BF16)
        RepN = sb("RepN_s", [128, D], BF16)
        RepT = sb("RepT_s", [128, 256], BF16)
        ident = sb("ident_s", [128, 128], BF16)
        onesK = sb("onesK_s", [128, 1], BF16)
        ones1f = sb("ones1f_s", [1, 128], F32)
        biasc = sb("biasc", [128, 32], F32)
        flushT = sb("flushT", [1, 8], F32)

        p = [psb(f"p{i}") for i in range(8)]

        sW = Ctr(ctx.enter_context(nc.semaphore("sW")), "sW")
        sC = Ctr(ctx.enter_context(nc.semaphore("sC")), "sC")
        sSt = Ctr(ctx.enter_context(nc.semaphore("sSt")), "sSt")
        sSt2 = Ctr(ctx.enter_context(nc.semaphore("sSt2")), "sSt2")
        sPE = Ctr(ctx.enter_context(nc.semaphore("sPE")), "sPE")
        sDV = Ctr(ctx.enter_context(nc.semaphore("sDV")), "sDV")
        sAC = Ctr(ctx.enter_context(nc.semaphore("sAC")), "sAC")
        sPL = Ctr(ctx.enter_context(nc.semaphore("sPL")), "sPL")

        def dve_flush():
            E("vector", lambda eng: eng.tensor_copy(out=flushT[0:1, 0:4],
                                                    in_=flushT[0:1, 4:8]), inc=sDV)
            return (sDV, sDV.n)

        def act_flush():
            E("scalar", lambda eng: eng.activation(out=flushT[0:1, 0:4],
                                                   in_=flushT[0:1, 4:8],
                                                   func=AF.Identity,
                                                   bias=np.float32(0.0),
                                                   scale=np.float32(1.0)), inc=sAC)
            return (sAC, sAC.n)

        def dma(queue, out_ap, in_ap, ctr, waits=()):
            return E(queue, (lambda eng, o=out_ap, i=in_ap: eng.dma_start(out=o, in_=i)),
                     waits, inc=ctr, k=16)

        def quad_ap(handle, gg, kq, ncols=512):
            off = (4 * kq) * 128 * D + 512 * gg
            return bass.AP(handle, off, [[D, 128], [128 * D, 4], [1, ncols]])

        def res_ch(ncx):
            return res[:, 512 * ncx:512 * ncx + 512]

        def lnT_ch(kc):
            return lnT[:, 512 * kc:512 * kc + 512]

        wtile_free = [[], [], []]
        tileinc = {}
        wstage_free = [[], []]
        wcast_free = [[], [], []]
        wt2c_done = {}
        wmc_done = {}
        atn_cv = {}

        E("vector", lambda eng: eng.memset(flushT[:, :], 0.0), inc=sDV)
        E("scalar", lambda eng: eng.memset(flushT[:, :], 0.0) if False else
          eng.activation(out=flushT[0:1, 0:8], in_=flushT[0:1, 0:8],
                         func=AF.Identity, bias=np.float32(0.0),
                         scale=np.float32(0.0)), [(sDV, sDV.n)], inc=sAC)
        dma("sync", RepN[:, :], RepN_d.ap(), sW)
        dma("sync", RepT[:, :], RepT_d.ap(), sW)
        dma("sync", ident[:, :], ident_d.ap(), sW)
        dma("sync", onesK[:, :], onesK_d.ap(), sW)
        dma("sync", ones1f[:, :], ones1f_d.ap(), sW)
        dma("sync", biasc[:, :], bias_r.ap(), sW)
        const_w = sW.n

        mk = {"cast": 0, "mm": 0, "pair_pe": [None] * 128, "cast_v": [None] * 128,
              "evac_fl": [None] * 8, "zt_done": None}

        def mask_cast_one():
            pr = mk["cast"]
            if pr >= 128:
                return
            s, q = divmod(pr, 16)
            mb_i = pr % 4
            waits = [(sW, const_w)]
            if pr >= 4:
                assert mk["pair_pe"][pr - 4] is not None, pr
                waits.append((sPE, mk["pair_pe"][pr - 4]))
            src = bass.AP(maskd, (2 * q) * 128 * D + 512 * s,
                          [[D, 128], [128 * D, 2], [1, 512]])
            dst = mtile[mb_i][:, :].rearrange("p (u k) -> p u k", u=2)
            dma("gpsimd", dst, src, sC, waits)
            mk["cast_v"][pr] = sC.n
            mk["cast"] += 1

        def mask_mm_one():
            pr = mk["mm"]
            if pr >= mk["cast"] or pr >= 128:
                return
            s, q = divmod(pr, 16)
            mb_i = pr % 4
            for u in range(2):
                mt = 2 * q + u
                cw = mk["cast_v"][pr + 1] if pr + 1 < 128 and mk["cast_v"][pr + 1] \
                    else mk.get("tailcast") or mk["cast_v"][pr]
                waits = [(sC, cw)]
                if q == 0 and u == 0 and s > 0:
                    waits.append(mk["evac_fl"][s - 1])

                def mm(eng, mb_i=mb_i, u=u, mt=mt):
                    return eng.matmul(p[4][:, :], RepT[:, 124 - 4 * mt:124 - 4 * mt + 128],
                                      mtile[mb_i][:, 512 * u:512 * u + 512],
                                      start=(mt == 0), stop=(mt == 31),
                                      skip_group_check=True)
                E("tensor", mm, waits, inc=sPE)
            mk["pair_pe"][pr] = sPE.n
            if q == 15:
                def evac(eng, s=s):
                    return eng.tensor_copy(out=zrow[:, 512 * s:512 * s + 512],
                                           in_=p[4][:, :])
                E("vector", evac, [(sPE, sPE.n)], inc=sDV)
                mk["evac_fl"][s] = dve_flush()
            mk["mm"] += 1

        def mask_flush():
            while mk["cast"] < 128:
                mask_cast_one()
                while mk["mm"] < mk["cast"] - 4:
                    mask_mm_one()
            if mk.get("tailcast") is None:
                def_src = bass.AP(maskd, 0, [[D, 128], [1, 64]])
                dma("gpsimd", mtile[(128) % 4][:, 0:64], def_src, sC,
                    [(sPE, mk["pair_pe"][124])] if mk["pair_pe"][124] else [])
                mk["tailcast"] = sC.n
            while mk["mm"] < 128:
                mask_mm_one()

        def emit_mask_stage2_zt():
            ev_fl = {}
            s2mm = {}
            for kc in range(NCH):
                bti = kc % 2

                def tr(eng, kc=kc):
                    return eng.matmul(p[5][:, 0:128], zrow[:, 128 * kc:128 * kc + 128],
                                      ident[:, :], start=True, stop=True,
                                      skip_group_check=True)
                waits = [mk["evac_fl"][7]]
                if kc >= 1:
                    waits.append(ev_fl[kc - 1])
                E("tensor", tr, waits, inc=sPE)
                trv = sPE.n

                def ev(eng, bti=bti):
                    return eng.tensor_copy(out=r32T[bti][:, :], in_=p[5][:, 0:128])
                w2 = [(sPE, trv)]
                if kc >= 2:
                    w2.append((sPE, s2mm[kc - 2]))
                E("vector", ev, w2, inc=sDV)
                ev_fl[kc] = dve_flush()

                def mm2(eng, kc=kc, bti=bti):
                    return eng.matmul(p[6][:, 0:128], RepT[:, 124 - 4 * kc:124 - 4 * kc + 128],
                                      r32T[bti][:, :], start=(kc == 0), stop=(kc == 31),
                                      skip_group_check=True)
                E("tensor", mm2, [ev_fl[kc]], inc=sPE)
                s2mm[kc] = sPE.n

            def gt(eng):
                return eng.tensor_scalar(out=bmT[:, :], in0=p[6][:, 0:128],
                                         scalar1=0.0, scalar2=None, op0=ALU.is_gt)
            E("vector", gt, [(sPE, sPE.n)], inc=sDV)
            bm_fl = dve_flush()
            zev_fl = {}
            for kq in range(KQ):
                for j in range(4):
                    kc = 4 * kq + j

                    def zmm(eng, kc=kc, j=j):
                        return eng.matmul(p[7][:, 128 * j:128 * j + 128],
                                          RepN[:, 128 * kc:128 * kc + 128],
                                          bmT[:, :], start=True, stop=True,
                                          skip_group_check=True)
                    waits = [bm_fl]
                    if kq >= 1 and j == 0:
                        waits.append(zev_fl[kq - 1])
                    E("tensor", zmm, waits, inc=sPE)

                def zev(eng, kq=kq):
                    return eng.tensor_copy(out=zrow[:, 512 * kq:512 * kq + 512],
                                           in_=p[7][:, :])
                E("vector", zev, [(sPE, sPE.n)], inc=sDV)
                zev_fl[kq] = dve_flush()
            mk["zt_done"] = sDV.n

        def emit_ln(kind, resfin, pbfree=None):
            pbfree = pbfree or [[] for _ in range(8)]
            sq_v = [None] * NCH
            mmx_v = [0] * NCH
            mms_v = [0] * NCH
            for ncx in range(NCH):
                bx = ncx % 4

                def cp(eng, ncx=ncx, bx=bx):
                    return eng.tensor_copy(out=xb[bx][:, :], in_=res_ch(ncx))
                E("gpsimd", cp, [resfin[ncx]] +
                  ([(sPE, mmx_v[ncx - 3])] if ncx >= 4 else []), inc=sPL)
                xbv = sPL.n

                def mmx(eng, ncx=ncx, bx=bx):
                    return eng.matmul(p[0][0:1, :], onesK[:, :], xb[bx][:, :],
                                      start=(ncx == 0), stop=(ncx == 31),
                                      skip_group_check=True)
                E("tensor", mmx, [(sPL, xbv), (sW, const_w)] +
                  (pbfree[0] if ncx == 0 else []), inc=sPE)
                mmx_v[ncx] = sPE.n

                def sqf(eng, ncx=ncx, bx=bx):
                    return eng.activation(out=sq[bx][:, :], in_=res_ch(ncx),
                                          func=AF.Square, bias=np.float32(0.0),
                                          scale=np.float32(1.0))
                E("scalar", sqf, [resfin[ncx]] +
                  ([(sPE, mmx_v[ncx - 2])] if ncx >= 4 else []), inc=sAC)
                sq_v[ncx] = sAC.n
                if ncx >= 1:
                    def mms(eng, pncx=ncx - 1, pbx=(ncx - 1) % 4):
                        return eng.matmul(p[1][0:1, :], onesK[:, :], sq[pbx][:, :],
                                          start=(pncx == 0), stop=False,
                                          skip_group_check=True)
                    E("tensor", mms, [(sAC, sq_v[ncx])] +
                      (pbfree[1] if ncx == 1 else []), inc=sPE)
                    mms_v[ncx - 1] = sPE.n
            a_fl = act_flush()

            def mms_last(eng):
                return eng.matmul(p[1][0:1, :], onesK[:, :], sq[31 % 4][:, :],
                                  start=False, stop=True, skip_group_check=True)
            E("tensor", mms_last, [a_fl], inc=sPE)
            stat_pe = sPE.n
            E("vector", lambda eng: eng.tensor_copy(out=B1[:, :], in_=p[0][0:1, :]),
              [(sPE, stat_pe)], inc=sDV)
            E("vector", lambda eng: eng.tensor_copy(out=B2[:, :], in_=p[1][0:1, :]),
              inc=sDV)
            E("vector", lambda eng: eng.tensor_scalar(out=B1[:, :], in0=B1[:, :],
                                                      scalar1=1.0 / 4096.0, scalar2=None,
                                                      op0=ALU.mult), inc=sDV)
            E("vector", lambda eng: eng.tensor_tensor(out=B3[:, :], in0=B1[:, :],
                                                      in1=B1[:, :], op=ALU.mult),
              inc=sDV)
            E("vector", lambda eng: eng.scalar_tensor_tensor(out=B2[:, :], in0=B2[:, :],
                                                             scalar=1.0 / 4096.0,
                                                             in1=B3[:, :], op0=ALU.mult,
                                                             op1=ALU.subtract), inc=sDV)
            fz1 = dve_flush()
            E("scalar", lambda eng: eng.activation(out=B3[:, :], in_=B2[:, :], func=AF.Ln,
                                                   bias=np.float32(EPS),
                                                   scale=np.float32(1.0)),
              [fz1], inc=sAC)
            lb = LN256 if kind != "at" else 0.0
            E("scalar", lambda eng, lb=lb: eng.activation(out=B2[:, :], in_=B3[:, :],
                                                          func=AF.Exp,
                                                          bias=np.float32(lb),
                                                          scale=np.float32(-0.5)),
              inc=sAC)
            az1 = act_flush()
            E("vector", lambda eng: eng.tensor_tensor(out=B3[:, :], in0=B1[:, :],
                                                      in1=B2[:, :], op=ALU.mult),
              [az1], inc=sDV)
            fz2 = dve_flush()
            E("tensor", lambda eng: eng.matmul(p[2][:, :], ones1f[0:1, :], B2[0:1, :],
                                               start=True, stop=True,
                                               skip_group_check=True),
              [az1] + pbfree[2], inc=sPE)
            E("tensor", lambda eng: eng.matmul(p[3][:, :], ones1f[0:1, :], B3[0:1, :],
                                               start=True, stop=True,
                                               skip_group_check=True),
              [fz2] + pbfree[3], inc=sPE)
            bc_pe = sPE.n
            E("vector", lambda eng: eng.tensor_copy(out=Rb[:, :], in_=p[2][:, :]),
              [(sPE, bc_pe)], inc=sDV)
            E("vector", lambda eng: eng.tensor_copy(out=MRb[:, :], in_=p[3][:, :]),
              inc=sDV)
            fz3 = dve_flush()

            lnd_raw = [0] * NCH
            fA = [None] * NCH
            aiv = [None] * NCH
            vqv = [None] * NCH
            d2pl = [None] * NCH

            def stageA(k):
                bx = k % 2

                def pt1(eng, k=k, bx=bx):
                    return eng.tensor_tensor(out=t1b[bx][:, :], in0=res_ch(k),
                                             in1=Rb[:, :], op=ALU.mult)
                w = [fz3]
                if k >= 2:
                    w.append(fA[k - 2] if kind != "at" else (sDV, lnd_raw[k - 1]))
                E("gpsimd", pt1, w, inc=sPL)
                t1v = sPL.n
                if kind == "at":
                    def app(eng, k=k, bx=bx):
                        return eng.tensor_tensor(out=lnT_ch(k), in0=t1b[bx][:, :],
                                                 in1=MRb[:, :], op=ALU.subtract)
                    E("vector", app, [(sPL, t1v)], inc=sDV)
                    lnd_raw[k] = sDV.n
                else:
                    def tt2(eng, k=k, bx=bx):
                        return eng.tensor_tensor(out=tv[bx][:, :], in0=t1b[bx][:, :],
                                                 in1=MRb[:, :], op=ALU.subtract)
                    wtt = [(sPL, t1v)]
                    if k >= 1 and aiv[k - 1] is not None:
                        wtt.append((sAC, aiv[k - 1]))
                    E("vector", tt2, wtt, inc=sDV)
                    fA[k] = dve_flush()

            def stageB(k):
                bx = k % 2

                def ai0(eng, bx=bx):
                    return eng.activation(out=i0b[:, :], in_=tv[bx][:, :],
                                          func=AF.Identity, bias=np.float32(0.0),
                                          scale=np.float32(1.0))
                w = [fA[k]]
                if k >= 2 and vqv[k - 2] is not None:
                    w.append((sDV, vqv[k - 2]))
                E("scalar", ai0, w, inc=sAC)
                aiv[k] = sAC.n

                def ar0(eng, bx=bx):
                    return eng.activation(out=r0b[bx][:, :], in_=i0b[:, :],
                                          func=AF.Identity, bias=np.float32(0.0),
                                          scale=np.float32(1.0))
                E("scalar", ar0, [], inc=sAC)

            def stageC(k, asafe):
                bx = k % 2

                def pd2(eng, bx=bx):
                    return eng.tensor_tensor(out=d2b[bx][:, :], in0=r0b[bx][:, :],
                                             in1=tv[bx][:, :], op=ALU.subtract)
                E("vector", pd2, [asafe], inc=sDV)

                def vb(eng, bx=bx):
                    return eng.tensor_scalar(out=b1b[:, :], in0=d2b[bx][:, :],
                                             scalar1=0.5, scalar2=None, op0=ALU.is_gt)
                E("vector", vb, [], inc=sDV)

                def vt(eng, bx=bx):
                    return eng.scalar_tensor_tensor(out=tqb[:, :], in0=d2b[bx][:, :],
                                                    scalar=-0.5, in1=r0b[bx][:, :],
                                                    op0=ALU.is_le, op1=ALU.add)
                E("vector", vt, [], inc=sDV)

                def vq(eng, k=k):
                    return eng.tensor_tensor(out=lnT_ch(k), in0=tqb[:, :],
                                             in1=b1b[:, :], op=ALU.subtract)
                E("vector", vq, [], inc=sDV)
                lnd_raw[k] = sDV.n
                vqv[k] = sDV.n

            if kind == "at":
                for k in range(NCH):
                    stageA(k)
                fin = dve_flush()
            else:
                for k in range(NCH):
                    if k >= 1:
                        stageB(k - 1)
                    if k >= 2:
                        stageC(k - 2, (sAC, aiv[k - 1]))
                    stageA(k)
                stageB(NCH - 1)
                stageC(NCH - 2, (sAC, aiv[NCH - 1]))
                afin = act_flush()
                stageC(NCH - 1, afin)
                fin = dve_flush()
            lnd_safe = [(sDV, lnd_raw[k + 1]) if k + 1 < NCH else fin
                        for k in range(NCH)]
            return lnd_safe

        def xb_sq_rhs(bx):
            return sq[bx][:, :]

        def resolve_tilew(waits):
            out = []
            for wv in waits:
                if isinstance(wv, tuple) and wv[0] == "tile":
                    t_old = wv[1]
                    nxt = t_old + 1
                    out.append((sPE, tileinc.get(nxt, tileinc[t_old])))
                else:
                    out.append(wv)
            return out

        pre_ready = {}

        def prep_f1_tile(t):
            gg, kq = divmod(t, 8)
            b = t % 3
            b2 = t % 2
            src = quad_ap(wf, gg, kq)
            dst = wstage[b2][:, :].rearrange("p (j n) -> p j n", j=4)
            qn, qc = ("sync", sW) if t % 2 == 0 else ("scalar", sSt2)
            dma(qn, dst, src, qc, wstage_free[b2])
            wv = (qc, qc.n)
            for hv in range(2):
                def tb1(eng, b2=b2, hv=hv):
                    return eng.tensor_scalar(
                        out=btern[:, :],
                        in0=wstage[b2][:, 1024 * hv:1024 * hv + 1024],
                        scalar1=-THR, scalar2=None, op0=ALU.is_lt)
                E("vector", tb1, [wv] +
                  (resolve_tilew(wtile_free[b]) if hv == 0 else []),
                  inc=sDV)

                def tb2(eng, b=b, b2=b2, hv=hv):
                    return eng.scalar_tensor_tensor(
                        out=wtile[b][:, 1024 * hv:1024 * hv + 1024],
                        in0=wstage[b2][:, 1024 * hv:1024 * hv + 1024],
                        scalar=THR, in1=btern[:, :],
                        op0=ALU.is_gt, op1=ALU.subtract)
                E("vector", tb2, [], inc=sDV)
            wready = dve_flush()
            wstage_free[b2] = [wready]
            sqn, scc = ("sync", sSt) if t % 2 == 1 else ("scalar", sSt2)
            dma(sqn, wt2c.ap()[t], wtile[b][:, :], scc, [wready])
            wt2c_done[t] = (scc, scc.n)
            return wready, (scc, scc.n)

        def prep_at_tile(t):
            gg, kq = divmod(t, 8)
            b = t % 3
            for tt in (t, t + 1, t + 2):
                if tt in atn_cv:
                    continue
                if tt < 64:
                    gg2, kq2 = divmod(tt, 8)
                    srcn = quad_ap(waT, gg2, kq2)
                    dstn = wcast[tt % 3][:, :].rearrange(
                        "p (j n) -> p j n", j=4)
                    dma("gpsimd", dstn, srcn, sC, wcast_free[tt % 3])
                    atn_cv[tt] = sC.n
                elif tt == 64:
                    dsrc = quad_ap(waT, 0, 0, ncols=64)
                    dma("gpsimd", wcast[tt % 3][:, 0:256].rearrange(
                        "p (j n) -> p j n", j=4), dsrc, sC,
                        wcast_free[tt % 3])
                    atn_cv[tt] = sC.n
            cv = atn_cv[t + 1]

            def mtt(eng, b=b, kq=kq, gg=gg):
                base = zrow[:, :]
                rep = bass.AP(base.tensor, base.offset + 512 * kq + 16 * gg,
                              [[D, 128], [128, 4], [1, 16], [0, 32]])
                return eng.tensor_tensor(
                    out=wtile[b][:, :].rearrange("p (j a c) -> p j a c",
                                                 j=4, a=16),
                    in0=wcast[b][:, :].rearrange("p (j a c) -> p j a c",
                                                 j=4, a=16),
                    in1=rep, op=ALU.mult)
            waits = [(sC, cv), (sDV, mk["zt_done"])] + \
                resolve_tilew(wtile_free[b])
            E("vector", mtt, waits, inc=sDV)
            wready = dve_flush()
            wcast_free[b] = [wready]
            dma("scalar", wmc.ap()[t], wtile[b][:, :], sSt2, [wready])
            wmc_done[t] = sSt2.n
            return wready, (sSt2, sSt2.n)

        def prep_tile(kind, h, t):
            if kind == "f1" and h == 0:
                return prep_f1_tile(t)
            if kind == "at" and h == 0:
                return prep_at_tile(t)
            b = t % 3
            if kind == "at":
                waits = resolve_tilew(wtile_free[b]) + [(sSt2, wmc_done[t])]
                dma("sync", wtile[b][:, :], wmc.ap()[t], sW, waits)
            else:
                waits = resolve_tilew(wtile_free[b])
                if t in wt2c_done:
                    waits.append(wt2c_done[t])
                dma("sync", wtile[b][:, :], wt2c.ap()[t], sW, waits)
            return (sW, sW.n), None

        def emit_groups(kind, h, lnd_safe, single_banked, with_mask=False):
            bank_free = [[] for _ in range(8)]
            res_mark = [None] * NCH
            epi = {}
            inter = (not single_banked) and not (kind == "at" and h == 0)
            sched = []
            if inter:
                for kq in range(KQ):
                    sched.append(("mm", 0, kq))
                    sched.append(("mm", 1, kq))
                sched.append(("epi", 0, 0))
                sched.append(("epi", 1, 0))
                for gg in range(2, G):
                    for kq in range(KQ):
                        sched.append(("mm", gg, kq))
                    sched.append(("epi", gg, 0))
            else:
                for gg in range(G):
                    for kq in range(KQ):
                        sched.append(("mm", gg, kq))
                    sched.append(("epi", gg, 0))
            tile_order = [gg * 8 + kq for (op, gg, kq) in sched if op == "mm"]
            grp_pe = {}

            def do_step(gg, kq, nxt_tile):
                t = gg * 8 + kq
                b = t % 3
                bs = 0 if (single_banked or gg % 2 == 0) else 4
                if with_mask:
                    if t >= 2:
                        mask_mm_one()
                        mask_mm_one()
                    mask_cast_one()
                    mask_cast_one()
                if (kind, h, t) in pre_ready:
                    wready, store_w = pre_ready.pop((kind, h, t))
                else:
                    wready, store_w = prep_tile(kind, h, t)
                early = (gg == 0 or (inter and gg == 1))
                for j in range(4):
                    for i in range(4):
                        kc = 4 * kq + j
                        first = (kq == 0 and j == 0)
                        laststop = (kq == KQ - 1 and j == 3)

                        def mm(eng, i=i, j=j, b=b, kc=kc, bs=bs, first=first,
                               laststop=laststop):
                            return eng.matmul(
                                p[bs + i][:, :],
                                wtile[b][:, 512 * j + 128 * i:512 * j + 128 * i + 128],
                                lnT_ch(kc), start=first, stop=laststop,
                                skip_group_check=True)
                        waits = [wready]
                        if early and i == 0:
                            waits = waits + [lnd_safe[kc]]
                        elif first and not early:
                            waits = waits + [lnd_safe[NCH - 1]]
                        if first:
                            waits = waits + bank_free[bs + i]
                        inc = sPE if (j == 3 and i == 3) else None
                        E("tensor", mm, waits, inc=inc)
                tileinc[t] = sPE.n
                grp_pe[gg] = sPE.n
                wtile_free[b] = [(sPE, tileinc[t])] + \
                    ([store_w] if store_w else [])
                if nxt_tile is not None and (kind, h, nxt_tile) not in pre_ready:
                    pre_ready[(kind, h, nxt_tile)] = prep_tile(kind, h, nxt_tile)

            def emit_epi(gg, next_tiles):
                bs = 0 if (single_banked or gg % 2 == 0) else 4
                group_pe = grp_pe[gg]
                if kind == "at":
                    epv = [None] * 4
                    for i in range(4):
                        ncx = 4 * gg + i

                        def ep(eng, i=i, bs=bs, ncx=ncx):
                            return eng.scalar_tensor_tensor(
                                out=res_ch(ncx), in0=p[bs + i][:, :],
                                scalar=biasc[:, ncx:ncx + 1], in1=res_ch(ncx),
                                op0=ALU.add, op1=ALU.add)
                        E("vector", ep, [(sPE, group_pe), (sW, const_w)], inc=sDV)
                        epv[i] = sDV.n
                        res_mark[ncx] = (sDV, sDV.n)
                    gfl = dve_flush()
                    for i in range(4):
                        bank_free[bs + i] = [(sDV, epv[i + 1]) if i + 1 < 4 else gfl]
                else:
                    e0v = [None] * 4
                    d2v = [None] * 4
                    lastdv = [None]

                    def e_e0(i):
                        ei = i % 2

                        def e0(eng, i=i, bs=bs, ei=ei):
                            return eng.tensor_scalar(out=tv[ei][:, :],
                                                     in0=p[bs + i][:, :],
                                                     scalar1=32767.0, scalar2=-32768.0,
                                                     op0=ALU.min, op1=ALU.max)
                        w = [(sPE, group_pe)]
                        if i >= 2:
                            w.append((sPL, d2v[i - 2]))
                        E("vector", e0, w, inc=sDV)
                        e0v[i] = sDV.n
                        lastdv[0] = sDV.n

                    def e_B(i, dvsafe):
                        ei = i % 2

                        def ai0(eng, ei=ei):
                            return eng.activation(out=i0b[:, :], in_=tv[ei][:, :],
                                                  func=AF.Identity,
                                                  bias=np.float32(0.0),
                                                  scale=np.float32(1.0))
                        w = [dvsafe]
                        if i >= 2:
                            w.append((sPL, d2v[i - 2]))
                        E("scalar", ai0, w, inc=sAC)

                        def ar0(eng, ei=ei):
                            return eng.activation(out=r0b[ei][:, :], in_=i0b[:, :],
                                                  func=AF.Identity,
                                                  bias=np.float32(0.0),
                                                  scale=np.float32(1.0))
                        E("scalar", ar0, [], inc=sAC)
                        return act_flush()

                    def e_C(i, asafe):
                        ei = i % 2
                        ncx = 4 * gg + i

                        def pd2(eng, ei=ei):
                            return eng.tensor_tensor(out=d2b[ei][:, :],
                                                     in0=r0b[ei][:, :],
                                                     in1=tv[ei][:, :], op=ALU.subtract)
                        E("gpsimd", pd2, [asafe] + epi.get(f"d2w{ei}", []), inc=sPL)
                        d2v[i] = sPL.n

                        def vb(eng, ei=ei):
                            return eng.tensor_scalar(out=b1b[:, :], in0=d2b[ei][:, :],
                                                     scalar1=0.5, scalar2=None,
                                                     op0=ALU.is_gt)
                        E("vector", vb, [(sPL, d2v[i])], inc=sDV)

                        def vt(eng, ei=ei):
                            return eng.scalar_tensor_tensor(out=tqb[:, :],
                                                            in0=d2b[ei][:, :],
                                                            scalar=-0.5,
                                                            in1=r0b[ei][:, :],
                                                            op0=ALU.is_le, op1=ALU.add)
                        E("vector", vt, [], inc=sDV)

                        def pq(eng):
                            return eng.tensor_tensor(out=qqb[:, :], in0=tqb[:, :],
                                                     in1=b1b[:, :], op=ALU.subtract)
                        E("vector", pq, [], inc=sDV)

                        def vr(eng, ncx=ncx):
                            return eng.scalar_tensor_tensor(out=res_ch(ncx),
                                                            in0=qqb[:, :],
                                                            scalar=1.0 / 512.0,
                                                            in1=res_ch(ncx),
                                                            op0=ALU.mult, op1=ALU.add)
                        E("vector", vr, [], inc=sDV)
                        epi[f"d2w{ei}"] = [(sDV, sDV.n)]
                        lastdv[0] = sDV.n
                        res_mark[ncx] = (sDV, sDV.n)

                    e_e0(0)
                    e_e0(1)
                    for tn in next_tiles:
                        if (kind, h, tn) not in pre_ready:
                            pre_ready[(kind, h, tn)] = prep_tile(kind, h, tn)
                            break
                    af0 = e_B(0, (sDV, e0v[1]))
                    e_C(0, af0)
                    e_e0(2)
                    af1 = e_B(1, (sDV, e0v[2]))
                    e_C(1, af1)
                    e_e0(3)
                    af2 = e_B(2, (sDV, e0v[3]))
                    e_C(2, af2)
                    gfl = dve_flush()
                    af3 = e_B(3, gfl)
                    e_C(3, af3)
                    gfl2 = dve_flush()
                    for i in range(4):
                        bank_free[bs + i] = [(sDV, e0v[i + 1]) if i + 1 < 4 else gfl2]

            for si, (op, gg, kq) in enumerate(sched):
                if op == "mm":
                    nxt_tile = None
                    for (op2, g2, k2) in sched[si + 1:]:
                        if op2 == "mm":
                            nxt_tile = g2 * 8 + k2
                            break
                        if kind != "at":
                            break
                    do_step(gg, kq, nxt_tile)
                else:
                    upcoming = [g2 * 8 + k2 for (op2, g2, k2) in sched[si + 1:]
                                if op2 == "mm"][:2]
                    emit_epi(gg, upcoming)
            if with_mask:
                mask_flush()
            return res_mark, [list(bank_free[i]) for i in range(8)]

        store_done = 0
        pbfree = None
        for h in (0, 1):
            hunits = [(hh, k) for (hh, k) in units if hh == h]
            if not hunits:
                continue
            waits = [(sW, const_w)]
            if h == 1:
                waits.append((sSt, store_done))
            resfin = [None] * NCH
            for ci in range(4):
                src = bass.AP(xT, 512 * h + 8 * ci * 128 * 2 * TH,
                              [[2 * TH, 128], [128 * 2 * TH, 8], [1, 512]])
                dst = res[:, 512 * 8 * ci:512 * 8 * (ci + 1)].rearrange(
                    "p (f t) -> p f t", f=8)
                dma("sync", dst, src, sW, waits if ci == 0 else [])
                for ncx in range(8 * ci, 8 * ci + 8):
                    resfin[ncx] = (sW, sW.n)
            first_ln = [True]
            for (hh, kind) in hunits:
                if kind == "at" and h == 0:
                    if dump:
                        dma("sync", dbg_rw.ap(), zrow[:, :], sSt, [mk["evac_fl"][7]])
                    emit_mask_stage2_zt()
                    if dump:
                        fl0 = dve_flush()
                        dma("sync", dbg_bm.ap(), bmT[:, :], sSt, [fl0])
                        dma("sync", dbg_zt.ap(), zrow[:, :], sSt, [fl0])
                single = (kind == "f1" and h == 0 and have_attn)
                inter_ph = (not single) and not (kind == "at" and h == 0)
                for t in ([0, 8, 1] if inter_ph else [0, 1, 2]):
                    if (kind, h, t) not in pre_ready:
                        pre_ready[(kind, h, t)] = prep_tile(kind, h, t)
                lnd_safe = emit_ln(kind, resfin, pbfree)
                if dump and h == 0 and first_ln[0]:
                    first_ln[0] = False
                    dma("sync", dbg_zt.ap(), lnT[:, 0:4096], sSt, [lnd_safe[NCH - 1]])
                single = (kind == "f1" and h == 0 and have_attn)
                wm = (kind == "f1" and h == 0 and have_attn)
                res_mark, pbfree = emit_groups(kind, h, lnd_safe, single, with_mask=wm)
                flg = dve_flush()
                resfin = [res_mark[i + 1] if i + 1 < NCH else flg for i in range(NCH)]
            dstd = bass.AP(outT, 512 * h, [[2 * TH, 128], [128 * 2 * TH, NCH], [1, 512]])
            dma("sync", dstd, res[:, :].rearrange("p (f t) -> p f t", f=NCH), sSt,
                [max(res_mark, key=lambda cv: cv[1]), dve_flush()])
            store_done = sSt.n
        if parts < 3:
            src = bass.AP(xT, 512, [[2 * TH, 128], [128 * 2 * TH, NCH], [1, 512]])
            dst = res[:, :].rearrange("p (f t) -> p f t", f=NCH)
            dma("sync", dst, src, sW, [(sSt, store_done)])
            dstd = bass.AP(outT, 512, [[2 * TH, 128], [128 * 2 * TH, NCH], [1, 512]])
            dma("sync", dstd, res[:, :].rearrange("p (f t) -> p f t", f=NCH), sSt,
                [(sW, sW.n)])
            store_done = sSt.n

        final_st = sSt.n
        E("sync", lambda eng: None, [(sSt, final_st)])

        block = ctx.enter_context(nc.Block())

        def replay(eng_name):
            def body(eng):
                for e, f in plan:
                    if e == eng_name:
                        f(eng)
            return body
        block.sync(replay("sync"))
        block.gpsimd(replay("gpsimd"))
        block.scalar(replay("scalar"))
        block.vector(replay("vector"))
        block.tensor(replay("tensor"))
    return nc


def _consts():
    a = np.arange(128)[:, None]
    j = np.arange(D)[None, :]
    RepN = (a == (j // 32)).astype(BF)
    mt = j // 128
    nb = j % 128
    y = np.arange(256)[None, :]
    RepT = (y == a // 32 + 124).astype(BF)
    ident = np.eye(128, dtype=np.float32).astype(BF)
    onesK = np.ones((128, 1), dtype=np.float32).astype(BF)
    ones1f = np.ones((1, 128), dtype=np.float32)
    return RepN, RepT, ident, onesK, ones1f


def kernel(L0_attn_data, L0_attn_mask, L0_attn_weight, L0_attn_bias, L0_ffn_w, _parts=6):
    X = np.ascontiguousarray(np.asarray(L0_attn_data, dtype=np.float32)).reshape(8192, D)
    mask = np.ascontiguousarray(np.asarray(L0_attn_mask, dtype=np.int32))
    waTh = np.ascontiguousarray(np.asarray(L0_attn_weight, dtype=np.float32).T)
    wfh = np.ascontiguousarray(np.asarray(L0_ffn_w, dtype=np.float32))
    bias_rh = np.ascontiguousarray(
        np.asarray(L0_attn_bias, dtype=np.float32).reshape(32, 128).T)
    RepN, RepT, ident, onesK, ones1f = _consts()

    nc = build_program(parts=_parts)
    in_maps = []
    for c in range(8):
        xTc = np.ascontiguousarray(X[1024 * c:1024 * (c + 1)].T)
        in_maps.append(dict(xT=xTc, maskd=mask, waT=waTh, wf=wfh, bias_r=bias_rh,
                            RepN=RepN, RepT=RepT, ident=ident, onesK=onesK,
                            ones1f=ones1f))
    res = run_bass_kernel_spmd(nc, in_maps, core_ids=list(range(8)))
    O = np.empty((8192, D), dtype=np.float32)
    for c in range(8):
        O[1024 * c:1024 * (c + 1)] = np.asarray(res.results[c]["outT"]).T
    return O.reshape(4, 2048, D)


# version 32 (final)
# speedup vs baseline: 1.1952x; 1.1952x over previous
import numpy as np
import ml_dtypes
import concourse.bass as bass
import concourse.mybir as mybir
from concourse.bass_utils import run_bass_kernel_spmd
from contextlib import ExitStack

F32 = mybir.dt.float32
BF16 = mybir.dt.bfloat16
I32 = mybir.dt.int32
AF = mybir.ActivationFunctionType
ALU = mybir.AluOpType
BF = ml_dtypes.bfloat16

D = 4096
NCH = 32
TH = 512
G = 8
KQ = 8
THR = 0.05
EPS = 1e-5
LN256 = float(np.log(256.0))


class Ctr:
    __slots__ = ("sem", "n", "name")

    def __init__(self, sem, name):
        self.sem = sem
        self.n = 0
        self.name = name


def build_program(parts=6, dump=False):
    nc = bass.Bass("TRN2", target_bir_lowering=False,
                   detect_race_conditions=False)

    xT = nc.dram_tensor("xT", [D, 2 * TH], F32, kind="ExternalInput")
    maskd = nc.dram_tensor("maskd", [D, D], I32, kind="ExternalInput")
    waT = nc.dram_tensor("waT", [D, D], F32, kind="ExternalInput")
    wf = nc.dram_tensor("wf", [D, D], F32, kind="ExternalInput")
    bias_r = nc.dram_tensor("bias_r", [128, 32], F32, kind="ExternalInput")
    RepN_d = nc.dram_tensor("RepN", [128, D], BF16, kind="ExternalInput")
    RepT_d = nc.dram_tensor("RepT", [128, 256], BF16, kind="ExternalInput")
    ident_d = nc.dram_tensor("ident", [128, 128], BF16, kind="ExternalInput")
    onesK_d = nc.dram_tensor("onesK", [128, 1], BF16, kind="ExternalInput")
    ones1f_d = nc.dram_tensor("ones1f", [1, 128], F32, kind="ExternalInput")
    outT = nc.dram_tensor("outT", [D, 2 * TH], F32, kind="ExternalOutput")
    dbg_bm = nc.dram_tensor("dbg_bm", [128, 128], BF16, kind="ExternalOutput") if dump else None
    dbg_zt = nc.dram_tensor("dbg_zt", [128, D], BF16, kind="ExternalOutput") if dump else None
    dbg_rw = nc.dram_tensor("dbg_rw", [128, D], BF16, kind="ExternalOutput") if dump else None
    wt2c = nc.dram_tensor("wt2c", [64, 128, 2048], BF16, kind="Internal")
    wmc = nc.dram_tensor("wmc", [64, 128, 2048], BF16, kind="Internal")

    units_all = [(0, "f1"), (0, "at"), (0, "f3"), (1, "f1"), (1, "at"), (1, "f3")]
    units = units_all[:parts]
    have_attn = any(k == "at" for _, k in units)

    plan = []
    last_wait = {}

    def E(engine, emitfn, waits=(), inc=None, k=1):
        w = []
        for (c, v) in waits:
            assert v is not None, "None wait value"
            if v <= 0:
                continue
            assert v <= c.n, f"wait on {c.name} for {v} but only {c.n} emitted"
            key = (engine, id(c))
            if last_wait.get(key, 0) >= v:
                continue
            last_wait[key] = v
            w.append((c.sem, v))
        tgt = None
        if inc is not None:
            inc.n += k
            tgt = (inc.sem, k)

        def f(eng, w=w, emitfn=emitfn, tgt=tgt):
            for s, v in w:
                eng.wait_ge(s, v)
            inst = emitfn(eng)
            if tgt is not None and inst is not None:
                inst.then_inc(*tgt)
        plan.append((engine, f))
        return inc.n if inc is not None else None

    with ExitStack() as ctx:
        def sb(name, shape, dtype):
            return ctx.enter_context(nc.sbuf_tensor(name, shape, dtype))

        def psb(name):
            return ctx.enter_context(nc.psum_tensor(name, [128, 512], F32))

        res = sb("res", [128, NCH * 512], F32)
        lnT = sb("lnT", [128, NCH * 512], BF16)
        wstage = [sb(f"wstage{i}", [128, 2048], F32) for i in range(2)]
        wtile = [sb(f"wtile{i}", [128, 2048], BF16) for i in range(3)]
        wcast = [sb(f"wcast{i}", [128, 2048], BF16) for i in range(3)]
        mtile = [sb(f"mtile{i}", [128, 1024], BF16) for i in range(4)]
        btern = sb("btern", [128, 1024], BF16)
        xb = [sb(f"xb{i}", [128, 512], BF16) for i in range(4)]
        sq = [sb(f"sq{i}", [128, 512], BF16) for i in range(4)]
        t1b = [sb(f"t1b{i}", [128, 512], F32) for i in range(2)]
        tv = [sb(f"tv{i}", [128, 512], F32) for i in range(2)]
        i0b = sb("i0b", [128, 512], I32)
        r0b = [sb(f"r0b{i}", [128, 512], F32) for i in range(2)]
        d2b = [sb(f"d2b{i}", [128, 512], F32) for i in range(2)]
        b1b = sb("b1b0", [128, 512], F32)
        tqb = sb("tqb0", [128, 512], F32)
        qqb = sb("qqb0", [128, 512], F32)
        Rb = sb("Rb", [128, 512], F32)
        MRb = sb("MRb", [128, 512], F32)
        B1 = sb("B1", [1, 512], F32)
        B2 = sb("B2", [1, 512], F32)
        B3 = sb("B3", [1, 512], F32)
        zrow = sb("zrow", [128, D], BF16)
        r32T = [sb(f"r32T{i}", [128, 128], BF16) for i in range(2)]
        bmT = sb("bmT", [128, 128], BF16)
        RepN = sb("RepN_s", [128, D], BF16)
        RepT = sb("RepT_s", [128, 256], BF16)
        ident = sb("ident_s", [128, 128], BF16)
        onesK = sb("onesK_s", [128, 1], BF16)
        ones1f = sb("ones1f_s", [1, 128], F32)
        biasc = sb("biasc", [128, 32], F32)
        flushT = sb("flushT", [1, 8], F32)

        p = [psb(f"p{i}") for i in range(8)]

        sW = Ctr(ctx.enter_context(nc.semaphore("sW")), "sW")
        sC = Ctr(ctx.enter_context(nc.semaphore("sC")), "sC")
        sSt = Ctr(ctx.enter_context(nc.semaphore("sSt")), "sSt")
        sSt2 = Ctr(ctx.enter_context(nc.semaphore("sSt2")), "sSt2")
        sPE = Ctr(ctx.enter_context(nc.semaphore("sPE")), "sPE")
        sDV = Ctr(ctx.enter_context(nc.semaphore("sDV")), "sDV")
        sAC = Ctr(ctx.enter_context(nc.semaphore("sAC")), "sAC")
        sPL = Ctr(ctx.enter_context(nc.semaphore("sPL")), "sPL")

        def dve_flush():
            E("vector", lambda eng: eng.tensor_copy(out=flushT[0:1, 0:4],
                                                    in_=flushT[0:1, 4:8]), inc=sDV)
            return (sDV, sDV.n)

        def act_flush():
            E("scalar", lambda eng: eng.activation(out=flushT[0:1, 0:4],
                                                   in_=flushT[0:1, 4:8],
                                                   func=AF.Identity,
                                                   bias=np.float32(0.0),
                                                   scale=np.float32(1.0)), inc=sAC)
            return (sAC, sAC.n)

        def dma(queue, out_ap, in_ap, ctr, waits=()):
            return E(queue, (lambda eng, o=out_ap, i=in_ap: eng.dma_start(out=o, in_=i)),
                     waits, inc=ctr, k=16)

        def quad_ap(handle, gg, kq, ncols=512):
            off = (4 * kq) * 128 * D + 512 * gg
            return bass.AP(handle, off, [[D, 128], [128 * D, 4], [1, ncols]])

        def res_ch(ncx):
            return res[:, 512 * ncx:512 * ncx + 512]

        def lnT_ch(kc):
            return lnT[:, 512 * kc:512 * kc + 512]

        wtile_free = [[], [], []]
        tileinc = {}
        wstage_free = [[], []]
        wcast_free = [[], [], []]
        wt2c_done = {}
        wmc_done = {}
        atn_cv = {}

        E("vector", lambda eng: eng.memset(flushT[:, :], 0.0), inc=sDV)
        E("scalar", lambda eng: eng.memset(flushT[:, :], 0.0) if False else
          eng.activation(out=flushT[0:1, 0:8], in_=flushT[0:1, 0:8],
                         func=AF.Identity, bias=np.float32(0.0),
                         scale=np.float32(0.0)), [(sDV, sDV.n)], inc=sAC)
        dma("sync", RepN[:, :], RepN_d.ap(), sW)
        dma("sync", RepT[:, :], RepT_d.ap(), sW)
        dma("sync", ident[:, :], ident_d.ap(), sW)
        dma("sync", onesK[:, :], onesK_d.ap(), sW)
        dma("sync", ones1f[:, :], ones1f_d.ap(), sW)
        dma("sync", biasc[:, :], bias_r.ap(), sW)
        const_w = sW.n

        mk = {"cast": 0, "mm": 0, "pair_pe": [None] * 128, "cast_v": [None] * 128,
              "evac_fl": [None] * 8, "zt_done": None}

        def mask_cast_one():
            pr = mk["cast"]
            if pr >= 128:
                return
            s, q = divmod(pr, 16)
            mb_i = pr % 4
            waits = [(sW, const_w)]
            if pr >= 4:
                assert mk["pair_pe"][pr - 4] is not None, pr
                waits.append((sPE, mk["pair_pe"][pr - 4]))
            src = bass.AP(maskd, (2 * q) * 128 * D + 512 * s,
                          [[D, 128], [128 * D, 2], [1, 512]])
            dst = mtile[mb_i][:, :].rearrange("p (u k) -> p u k", u=2)
            dma("gpsimd", dst, src, sC, waits)
            mk["cast_v"][pr] = sC.n
            mk["cast"] += 1

        def mask_mm_one():
            pr = mk["mm"]
            if pr >= mk["cast"] or pr >= 128:
                return
            s, q = divmod(pr, 16)
            mb_i = pr % 4
            for u in range(2):
                mt = 2 * q + u
                cw = mk["cast_v"][pr + 1] if pr + 1 < 128 and mk["cast_v"][pr + 1] \
                    else mk.get("tailcast") or mk["cast_v"][pr]
                waits = [(sC, cw)]
                if q == 0 and u == 0 and s > 0:
                    waits.append(mk["evac_fl"][s - 1])

                def mm(eng, mb_i=mb_i, u=u, mt=mt):
                    return eng.matmul(p[4][:, :], RepT[:, 124 - 4 * mt:124 - 4 * mt + 128],
                                      mtile[mb_i][:, 512 * u:512 * u + 512],
                                      start=(mt == 0), stop=(mt == 31),
                                      skip_group_check=True)
                E("tensor", mm, waits, inc=sPE)
            mk["pair_pe"][pr] = sPE.n
            if q == 15:
                def evac(eng, s=s):
                    return eng.tensor_copy(out=zrow[:, 512 * s:512 * s + 512],
                                           in_=p[4][:, :])
                E("vector", evac, [(sPE, sPE.n)], inc=sDV)
                mk["evac_fl"][s] = dve_flush()
            mk["mm"] += 1

        def mask_flush():
            while mk["cast"] < 128:
                mask_cast_one()
                while mk["mm"] < mk["cast"] - 4:
                    mask_mm_one()
            if mk.get("tailcast") is None:
                def_src = bass.AP(maskd, 0, [[D, 128], [1, 64]])
                dma("gpsimd", mtile[(128) % 4][:, 0:64], def_src, sC,
                    [(sPE, mk["pair_pe"][124])] if mk["pair_pe"][124] else [])
                mk["tailcast"] = sC.n
            while mk["mm"] < 128:
                mask_mm_one()

        def emit_mask_stage2_zt():
            ev_fl = {}
            s2mm = {}
            for kc in range(NCH):
                bti = kc % 2

                def tr(eng, kc=kc):
                    return eng.matmul(p[5][:, 0:128], zrow[:, 128 * kc:128 * kc + 128],
                                      ident[:, :], start=True, stop=True,
                                      skip_group_check=True)
                waits = [mk["evac_fl"][7]]
                if kc >= 1:
                    waits.append(ev_fl[kc - 1])
                E("tensor", tr, waits, inc=sPE)
                trv = sPE.n

                def ev(eng, bti=bti):
                    return eng.tensor_copy(out=r32T[bti][:, :], in_=p[5][:, 0:128])
                w2 = [(sPE, trv)]
                if kc >= 2:
                    w2.append((sPE, s2mm[kc - 2]))
                E("vector", ev, w2, inc=sDV)
                ev_fl[kc] = dve_flush()

                def mm2(eng, kc=kc, bti=bti):
                    return eng.matmul(p[6][:, 0:128], RepT[:, 124 - 4 * kc:124 - 4 * kc + 128],
                                      r32T[bti][:, :], start=(kc == 0), stop=(kc == 31),
                                      skip_group_check=True)
                E("tensor", mm2, [ev_fl[kc]], inc=sPE)
                s2mm[kc] = sPE.n

            def gt(eng):
                return eng.tensor_scalar(out=bmT[:, :], in0=p[6][:, 0:128],
                                         scalar1=0.0, scalar2=None, op0=ALU.is_gt)
            E("vector", gt, [(sPE, sPE.n)], inc=sDV)
            bm_fl = dve_flush()
            zev_fl = {}
            for kq in range(KQ):
                for j in range(4):
                    kc = 4 * kq + j

                    def zmm(eng, kc=kc, j=j):
                        return eng.matmul(p[7][:, 128 * j:128 * j + 128],
                                          RepN[:, 128 * kc:128 * kc + 128],
                                          bmT[:, :], start=True, stop=True,
                                          skip_group_check=True)
                    waits = [bm_fl]
                    if kq >= 1 and j == 0:
                        waits.append(zev_fl[kq - 1])
                    E("tensor", zmm, waits, inc=sPE)

                def zev(eng, kq=kq):
                    return eng.tensor_copy(out=zrow[:, 512 * kq:512 * kq + 512],
                                           in_=p[7][:, :])
                E("vector", zev, [(sPE, sPE.n)], inc=sDV)
                zev_fl[kq] = dve_flush()
            mk["zt_done"] = sDV.n

        def emit_ln(kind, resfin, pbfree=None):
            pbfree = pbfree or [[] for _ in range(8)]
            sq_v = [None] * NCH
            mmx_v = [0] * NCH
            mms_v = [0] * NCH
            for ncx in range(NCH):
                bx = ncx % 4

                def cp(eng, ncx=ncx, bx=bx):
                    return eng.tensor_copy(out=xb[bx][:, :], in_=res_ch(ncx))
                E("gpsimd", cp, [resfin[ncx]] +
                  ([(sPE, mmx_v[ncx - 3])] if ncx >= 4 else []), inc=sPL)
                xbv = sPL.n

                def mmx(eng, ncx=ncx, bx=bx):
                    return eng.matmul(p[0][0:1, :], onesK[:, :], xb[bx][:, :],
                                      start=(ncx == 0), stop=(ncx == 31),
                                      skip_group_check=True)
                E("tensor", mmx, [(sPL, xbv), (sW, const_w)] +
                  (pbfree[0] if ncx == 0 else []), inc=sPE)
                mmx_v[ncx] = sPE.n

                def sqf(eng, ncx=ncx, bx=bx):
                    return eng.activation(out=sq[bx][:, :], in_=res_ch(ncx),
                                          func=AF.Square, bias=np.float32(0.0),
                                          scale=np.float32(1.0))
                E("scalar", sqf, [resfin[ncx]] +
                  ([(sPE, mmx_v[ncx - 2])] if ncx >= 4 else []), inc=sAC)
                sq_v[ncx] = sAC.n
                if ncx >= 1:
                    def mms(eng, pncx=ncx - 1, pbx=(ncx - 1) % 4):
                        return eng.matmul(p[1][0:1, :], onesK[:, :], sq[pbx][:, :],
                                          start=(pncx == 0), stop=False,
                                          skip_group_check=True)
                    E("tensor", mms, [(sAC, sq_v[ncx])] +
                      (pbfree[1] if ncx == 1 else []), inc=sPE)
                    mms_v[ncx - 1] = sPE.n
            a_fl = act_flush()

            def mms_last(eng):
                return eng.matmul(p[1][0:1, :], onesK[:, :], sq[31 % 4][:, :],
                                  start=False, stop=True, skip_group_check=True)
            E("tensor", mms_last, [a_fl], inc=sPE)
            stat_pe = sPE.n
            E("vector", lambda eng: eng.tensor_copy(out=B1[:, :], in_=p[0][0:1, :]),
              [(sPE, stat_pe)], inc=sDV)
            E("vector", lambda eng: eng.tensor_copy(out=B2[:, :], in_=p[1][0:1, :]),
              inc=sDV)
            E("vector", lambda eng: eng.tensor_scalar(out=B1[:, :], in0=B1[:, :],
                                                      scalar1=1.0 / 4096.0, scalar2=None,
                                                      op0=ALU.mult), inc=sDV)
            E("vector", lambda eng: eng.tensor_tensor(out=B3[:, :], in0=B1[:, :],
                                                      in1=B1[:, :], op=ALU.mult),
              inc=sDV)
            E("vector", lambda eng: eng.scalar_tensor_tensor(out=B2[:, :], in0=B2[:, :],
                                                             scalar=1.0 / 4096.0,
                                                             in1=B3[:, :], op0=ALU.mult,
                                                             op1=ALU.subtract), inc=sDV)
            fz1 = dve_flush()
            E("scalar", lambda eng: eng.activation(out=B3[:, :], in_=B2[:, :], func=AF.Ln,
                                                   bias=np.float32(EPS),
                                                   scale=np.float32(1.0)),
              [fz1], inc=sAC)
            lb = LN256 if kind != "at" else 0.0
            E("scalar", lambda eng, lb=lb: eng.activation(out=B2[:, :], in_=B3[:, :],
                                                          func=AF.Exp,
                                                          bias=np.float32(lb),
                                                          scale=np.float32(-0.5)),
              inc=sAC)
            az1 = act_flush()
            E("vector", lambda eng: eng.tensor_tensor(out=B3[:, :], in0=B1[:, :],
                                                      in1=B2[:, :], op=ALU.mult),
              [az1], inc=sDV)
            fz2 = dve_flush()
            E("tensor", lambda eng: eng.matmul(p[2][:, :], ones1f[0:1, :], B2[0:1, :],
                                               start=True, stop=True,
                                               skip_group_check=True),
              [az1] + pbfree[2], inc=sPE)
            E("tensor", lambda eng: eng.matmul(p[3][:, :], ones1f[0:1, :], B3[0:1, :],
                                               start=True, stop=True,
                                               skip_group_check=True),
              [fz2] + pbfree[3], inc=sPE)
            bc_pe = sPE.n
            E("vector", lambda eng: eng.tensor_copy(out=Rb[:, :], in_=p[2][:, :]),
              [(sPE, bc_pe)], inc=sDV)
            E("vector", lambda eng: eng.tensor_copy(out=MRb[:, :], in_=p[3][:, :]),
              inc=sDV)
            fz3 = dve_flush()

            lnd_raw = [0] * NCH
            fA = [None] * NCH
            aiv = [None] * NCH
            vqv = [None] * NCH
            d2pl = [None] * NCH

            def stageA(k):
                bx = k % 2

                def pt1(eng, k=k, bx=bx):
                    return eng.tensor_tensor(out=t1b[bx][:, :], in0=res_ch(k),
                                             in1=Rb[:, :], op=ALU.mult)
                w = [fz3]
                if k >= 2:
                    w.append(fA[k - 2] if kind != "at" else (sDV, lnd_raw[k - 1]))
                E("gpsimd", pt1, w, inc=sPL)
                t1v = sPL.n
                if kind == "at":
                    def app(eng, k=k, bx=bx):
                        return eng.tensor_tensor(out=lnT_ch(k), in0=t1b[bx][:, :],
                                                 in1=MRb[:, :], op=ALU.subtract)
                    E("vector", app, [(sPL, t1v)], inc=sDV)
                    lnd_raw[k] = sDV.n
                else:
                    def tt2(eng, k=k, bx=bx):
                        return eng.tensor_tensor(out=tv[bx][:, :], in0=t1b[bx][:, :],
                                                 in1=MRb[:, :], op=ALU.subtract)
                    wtt = [(sPL, t1v)]
                    if k >= 1 and aiv[k - 1] is not None:
                        wtt.append((sAC, aiv[k - 1]))
                    E("vector", tt2, wtt, inc=sDV)
                    fA[k] = dve_flush()

            def stageB(k):
                bx = k % 2

                def ai0(eng, bx=bx):
                    return eng.activation(out=i0b[:, :], in_=tv[bx][:, :],
                                          func=AF.Identity, bias=np.float32(0.0),
                                          scale=np.float32(1.0))
                w = [fA[k]]
                if k >= 2 and vqv[k - 2] is not None:
                    w.append((sDV, vqv[k - 2]))
                E("scalar", ai0, w, inc=sAC)
                aiv[k] = sAC.n

                def ar0(eng, bx=bx):
                    return eng.activation(out=r0b[bx][:, :], in_=i0b[:, :],
                                          func=AF.Identity, bias=np.float32(0.0),
                                          scale=np.float32(1.0))
                E("scalar", ar0, [], inc=sAC)

            def stageC(k, asafe):
                bx = k % 2

                def pd2(eng, bx=bx):
                    return eng.tensor_tensor(out=d2b[bx][:, :], in0=r0b[bx][:, :],
                                             in1=tv[bx][:, :], op=ALU.subtract)
                E("vector", pd2, [asafe], inc=sDV)

                def vb(eng, bx=bx):
                    return eng.tensor_scalar(out=b1b[:, :], in0=d2b[bx][:, :],
                                             scalar1=0.5, scalar2=None, op0=ALU.is_gt)
                E("vector", vb, [], inc=sDV)

                def vt(eng, bx=bx):
                    return eng.scalar_tensor_tensor(out=tqb[:, :], in0=d2b[bx][:, :],
                                                    scalar=-0.5, in1=r0b[bx][:, :],
                                                    op0=ALU.is_le, op1=ALU.add)
                E("vector", vt, [], inc=sDV)

                def vq(eng, k=k):
                    return eng.tensor_tensor(out=lnT_ch(k), in0=tqb[:, :],
                                             in1=b1b[:, :], op=ALU.subtract)
                E("vector", vq, [], inc=sDV)
                lnd_raw[k] = sDV.n
                vqv[k] = sDV.n

            if kind == "at":
                for k in range(NCH):
                    stageA(k)
                fin = dve_flush()
            else:
                for k in range(NCH):
                    if k >= 1:
                        stageB(k - 1)
                    if k >= 2:
                        stageC(k - 2, (sAC, aiv[k - 1]))
                    stageA(k)
                stageB(NCH - 1)
                stageC(NCH - 2, (sAC, aiv[NCH - 1]))
                afin = act_flush()
                stageC(NCH - 1, afin)
                fin = dve_flush()
            lnd_safe = [(sDV, lnd_raw[k + 1]) if k + 1 < NCH else fin
                        for k in range(NCH)]
            return lnd_safe

        def xb_sq_rhs(bx):
            return sq[bx][:, :]

        def resolve_tilew(waits):
            out = []
            for wv in waits:
                if isinstance(wv, tuple) and wv[0] == "tile":
                    t_old = wv[1]
                    nxt = t_old + 1
                    out.append((sPE, tileinc.get(nxt, tileinc[t_old])))
                else:
                    out.append(wv)
            return out

        pre_ready = {}

        def prep_f1_tile(t):
            gg, kq = divmod(t, 8)
            b = t % 3
            b2 = t % 2
            src = quad_ap(wf, gg, kq)
            dst = wstage[b2][:, :].rearrange("p (j n) -> p j n", j=4)
            qn, qc = ("sync", sW) if t % 2 == 0 else ("scalar", sSt2)
            dma(qn, dst, src, qc, wstage_free[b2])
            wv = (qc, qc.n)
            for hv in range(2):
                def tb1(eng, b2=b2, hv=hv):
                    return eng.tensor_scalar(
                        out=btern[:, :],
                        in0=wstage[b2][:, 1024 * hv:1024 * hv + 1024],
                        scalar1=-THR, scalar2=None, op0=ALU.is_lt)
                E("vector", tb1, [wv] +
                  (resolve_tilew(wtile_free[b]) if hv == 0 else []),
                  inc=sDV)

                def tb2(eng, b=b, b2=b2, hv=hv):
                    return eng.scalar_tensor_tensor(
                        out=wtile[b][:, 1024 * hv:1024 * hv + 1024],
                        in0=wstage[b2][:, 1024 * hv:1024 * hv + 1024],
                        scalar=THR, in1=btern[:, :],
                        op0=ALU.is_gt, op1=ALU.subtract)
                E("vector", tb2, [], inc=sDV)
            wready = dve_flush()
            wstage_free[b2] = [wready]
            dma("scalar", wt2c.ap()[t], wtile[b][:, :], sSt2, [wready])
            wt2c_done[t] = sSt2.n
            return wready, (sSt2, sSt2.n)

        def prep_at_tile(t):
            gg, kq = divmod(t, 8)
            b = t % 3
            for tt in (t, t + 1, t + 2):
                if tt in atn_cv:
                    continue
                if tt < 64:
                    gg2, kq2 = divmod(tt, 8)
                    srcn = quad_ap(waT, gg2, kq2)
                    dstn = wcast[tt % 3][:, :].rearrange(
                        "p (j n) -> p j n", j=4)
                    dma("gpsimd", dstn, srcn, sC, wcast_free[tt % 3])
                    atn_cv[tt] = sC.n
                elif tt == 64:
                    dsrc = quad_ap(waT, 0, 0, ncols=64)
                    dma("gpsimd", wcast[tt % 3][:, 0:256].rearrange(
                        "p (j n) -> p j n", j=4), dsrc, sC,
                        wcast_free[tt % 3])
                    atn_cv[tt] = sC.n
            cv = atn_cv[t + 1]

            def mtt(eng, b=b, kq=kq, gg=gg):
                base = zrow[:, :]
                rep = bass.AP(base.tensor, base.offset + 512 * kq + 16 * gg,
                              [[D, 128], [128, 4], [1, 16], [0, 32]])
                return eng.tensor_tensor(
                    out=wtile[b][:, :].rearrange("p (j a c) -> p j a c",
                                                 j=4, a=16),
                    in0=wcast[b][:, :].rearrange("p (j a c) -> p j a c",
                                                 j=4, a=16),
                    in1=rep, op=ALU.mult)
            waits = [(sC, cv), (sDV, mk["zt_done"])] + \
                resolve_tilew(wtile_free[b])
            E("vector", mtt, waits, inc=sDV)
            wready = dve_flush()
            wcast_free[b] = [wready]
            dma("scalar", wmc.ap()[t], wtile[b][:, :], sSt2, [wready])
            wmc_done[t] = sSt2.n
            return wready, (sSt2, sSt2.n)

        def prep_tile(kind, h, t):
            if kind == "f1" and h == 0:
                return prep_f1_tile(t)
            if kind == "at" and h == 0:
                return prep_at_tile(t)
            b = t % 3
            if kind == "at":
                waits = resolve_tilew(wtile_free[b]) + [(sSt2, wmc_done[t])]
                dma("sync", wtile[b][:, :], wmc.ap()[t], sW, waits)
            else:
                waits = resolve_tilew(wtile_free[b])
                if t in wt2c_done:
                    waits.append((sSt2, wt2c_done[t]))
                dma("sync", wtile[b][:, :], wt2c.ap()[t], sW, waits)
            return (sW, sW.n), None

        def emit_groups(kind, h, lnd_safe, single_banked, with_mask=False):
            bank_free = [[] for _ in range(8)]
            res_mark = [None] * NCH
            epi = {}
            inter = (not single_banked) and not (kind == "at" and h == 0)
            sched = []
            if inter:
                for kq in range(KQ):
                    sched.append(("mm", 0, kq))
                    sched.append(("mm", 1, kq))
                sched.append(("epi", 0, 0))
                sched.append(("epi", 1, 0))
                for gg in range(2, G):
                    for kq in range(KQ):
                        sched.append(("mm", gg, kq))
                    sched.append(("epi", gg, 0))
            else:
                for gg in range(G):
                    for kq in range(KQ):
                        sched.append(("mm", gg, kq))
                    sched.append(("epi", gg, 0))
            tile_order = [gg * 8 + kq for (op, gg, kq) in sched if op == "mm"]
            grp_pe = {}

            def do_step(gg, kq, nxt_tile):
                t = gg * 8 + kq
                b = t % 3
                bs = 0 if (single_banked or gg % 2 == 0) else 4
                if with_mask:
                    if t >= 2:
                        mask_mm_one()
                        mask_mm_one()
                    mask_cast_one()
                    mask_cast_one()
                if (kind, h, t) in pre_ready:
                    wready, store_w = pre_ready.pop((kind, h, t))
                else:
                    wready, store_w = prep_tile(kind, h, t)
                early = (gg == 0 or (inter and gg == 1))
                for j in range(4):
                    for i in range(4):
                        kc = 4 * kq + j
                        first = (kq == 0 and j == 0)
                        laststop = (kq == KQ - 1 and j == 3)

                        def mm(eng, i=i, j=j, b=b, kc=kc, bs=bs, first=first,
                               laststop=laststop):
                            return eng.matmul(
                                p[bs + i][:, :],
                                wtile[b][:, 512 * j + 128 * i:512 * j + 128 * i + 128],
                                lnT_ch(kc), start=first, stop=laststop,
                                skip_group_check=True)
                        waits = [wready]
                        if early and i == 0:
                            waits = waits + [lnd_safe[kc]]
                        elif first and not early:
                            waits = waits + [lnd_safe[NCH - 1]]
                        if first:
                            waits = waits + bank_free[bs + i]
                        inc = sPE if (j == 3 and i == 3) else None
                        E("tensor", mm, waits, inc=inc)
                tileinc[t] = sPE.n
                grp_pe[gg] = sPE.n
                wtile_free[b] = [(sPE, tileinc[t])] + \
                    ([store_w] if store_w else [])
                if nxt_tile is not None and (kind, h, nxt_tile) not in pre_ready:
                    pre_ready[(kind, h, nxt_tile)] = prep_tile(kind, h, nxt_tile)

            def emit_epi(gg, next_tiles):
                bs = 0 if (single_banked or gg % 2 == 0) else 4
                group_pe = grp_pe[gg]
                if kind == "at":
                    epv = [None] * 4
                    for i in range(4):
                        ncx = 4 * gg + i

                        def ep(eng, i=i, bs=bs, ncx=ncx):
                            return eng.scalar_tensor_tensor(
                                out=res_ch(ncx), in0=p[bs + i][:, :],
                                scalar=biasc[:, ncx:ncx + 1], in1=res_ch(ncx),
                                op0=ALU.add, op1=ALU.add)
                        E("vector", ep, [(sPE, group_pe), (sW, const_w)], inc=sDV)
                        epv[i] = sDV.n
                        res_mark[ncx] = (sDV, sDV.n)
                    gfl = dve_flush()
                    for i in range(4):
                        bank_free[bs + i] = [(sDV, epv[i + 1]) if i + 1 < 4 else gfl]
                else:
                    e0v = [None] * 4
                    d2v = [None] * 4
                    lastdv = [None]

                    def e_e0(i):
                        ei = i % 2

                        def e0(eng, i=i, bs=bs, ei=ei):
                            return eng.tensor_scalar(out=tv[ei][:, :],
                                                     in0=p[bs + i][:, :],
                                                     scalar1=32767.0, scalar2=-32768.0,
                                                     op0=ALU.min, op1=ALU.max)
                        w = [(sPE, group_pe)]
                        if i >= 2:
                            w.append((sPL, d2v[i - 2]))
                        E("vector", e0, w, inc=sDV)
                        e0v[i] = sDV.n
                        lastdv[0] = sDV.n

                    def e_B(i, dvsafe):
                        ei = i % 2

                        def ai0(eng, ei=ei):
                            return eng.activation(out=i0b[:, :], in_=tv[ei][:, :],
                                                  func=AF.Identity,
                                                  bias=np.float32(0.0),
                                                  scale=np.float32(1.0))
                        w = [dvsafe]
                        if i >= 2:
                            w.append((sPL, d2v[i - 2]))
                        E("scalar", ai0, w, inc=sAC)

                        def ar0(eng, ei=ei):
                            return eng.activation(out=r0b[ei][:, :], in_=i0b[:, :],
                                                  func=AF.Identity,
                                                  bias=np.float32(0.0),
                                                  scale=np.float32(1.0))
                        E("scalar", ar0, [], inc=sAC)
                        return act_flush()

                    def e_C(i, asafe):
                        ei = i % 2
                        ncx = 4 * gg + i

                        def pd2(eng, ei=ei):
                            return eng.tensor_tensor(out=d2b[ei][:, :],
                                                     in0=r0b[ei][:, :],
                                                     in1=tv[ei][:, :], op=ALU.subtract)
                        E("gpsimd", pd2, [asafe] + epi.get(f"d2w{ei}", []), inc=sPL)
                        d2v[i] = sPL.n

                        def vb(eng, ei=ei):
                            return eng.tensor_scalar(out=b1b[:, :], in0=d2b[ei][:, :],
                                                     scalar1=0.5, scalar2=None,
                                                     op0=ALU.is_gt)
                        E("vector", vb, [(sPL, d2v[i])], inc=sDV)

                        def vt(eng, ei=ei):
                            return eng.scalar_tensor_tensor(out=tqb[:, :],
                                                            in0=d2b[ei][:, :],
                                                            scalar=-0.5,
                                                            in1=r0b[ei][:, :],
                                                            op0=ALU.is_le, op1=ALU.add)
                        E("vector", vt, [], inc=sDV)

                        def pq(eng):
                            return eng.tensor_tensor(out=qqb[:, :], in0=tqb[:, :],
                                                     in1=b1b[:, :], op=ALU.subtract)
                        E("vector", pq, [], inc=sDV)

                        def vr(eng, ncx=ncx):
                            return eng.scalar_tensor_tensor(out=res_ch(ncx),
                                                            in0=qqb[:, :],
                                                            scalar=1.0 / 512.0,
                                                            in1=res_ch(ncx),
                                                            op0=ALU.mult, op1=ALU.add)
                        E("vector", vr, [], inc=sDV)
                        epi[f"d2w{ei}"] = [(sDV, sDV.n)]
                        lastdv[0] = sDV.n
                        res_mark[ncx] = (sDV, sDV.n)

                    e_e0(0)
                    e_e0(1)
                    for tn in next_tiles:
                        if (kind, h, tn) not in pre_ready:
                            pre_ready[(kind, h, tn)] = prep_tile(kind, h, tn)
                            break
                    af0 = e_B(0, (sDV, e0v[1]))
                    e_C(0, af0)
                    e_e0(2)
                    af1 = e_B(1, (sDV, e0v[2]))
                    e_C(1, af1)
                    e_e0(3)
                    af2 = e_B(2, (sDV, e0v[3]))
                    e_C(2, af2)
                    gfl = dve_flush()
                    af3 = e_B(3, gfl)
                    e_C(3, af3)
                    gfl2 = dve_flush()
                    for i in range(4):
                        bank_free[bs + i] = [(sDV, e0v[i + 1]) if i + 1 < 4 else gfl2]

            for si, (op, gg, kq) in enumerate(sched):
                if op == "mm":
                    nxt_tile = None
                    for (op2, g2, k2) in sched[si + 1:]:
                        if op2 == "mm":
                            nxt_tile = g2 * 8 + k2
                            break
                        if kind != "at":
                            break
                    do_step(gg, kq, nxt_tile)
                else:
                    upcoming = [g2 * 8 + k2 for (op2, g2, k2) in sched[si + 1:]
                                if op2 == "mm"][:2]
                    emit_epi(gg, upcoming)
            if with_mask:
                mask_flush()
            return res_mark, [list(bank_free[i]) for i in range(8)]

        store_done = 0
        pbfree = None
        for h in (0, 1):
            hunits = [(hh, k) for (hh, k) in units if hh == h]
            if not hunits:
                continue
            waits = [(sW, const_w)]
            if h == 1:
                waits.append((sSt, store_done))
            resfin = [None] * NCH
            for ci in range(4):
                src = bass.AP(xT, 512 * h + 8 * ci * 128 * 2 * TH,
                              [[2 * TH, 128], [128 * 2 * TH, 8], [1, 512]])
                dst = res[:, 512 * 8 * ci:512 * 8 * (ci + 1)].rearrange(
                    "p (f t) -> p f t", f=8)
                dma("sync", dst, src, sW, waits if ci == 0 else [])
                for ncx in range(8 * ci, 8 * ci + 8):
                    resfin[ncx] = (sW, sW.n)
            first_ln = [True]
            for (hh, kind) in hunits:
                if kind == "at" and h == 0:
                    if dump:
                        dma("sync", dbg_rw.ap(), zrow[:, :], sSt, [mk["evac_fl"][7]])
                    emit_mask_stage2_zt()
                    if dump:
                        fl0 = dve_flush()
                        dma("sync", dbg_bm.ap(), bmT[:, :], sSt, [fl0])
                        dma("sync", dbg_zt.ap(), zrow[:, :], sSt, [fl0])
                single = (kind == "f1" and h == 0 and have_attn)
                inter_ph = (not single) and not (kind == "at" and h == 0)
                for t in ([0, 8, 1] if inter_ph else [0, 1, 2]):
                    if (kind, h, t) not in pre_ready:
                        pre_ready[(kind, h, t)] = prep_tile(kind, h, t)
                lnd_safe = emit_ln(kind, resfin, pbfree)
                if dump and h == 0 and first_ln[0]:
                    first_ln[0] = False
                    dma("sync", dbg_zt.ap(), lnT[:, 0:4096], sSt, [lnd_safe[NCH - 1]])
                single = (kind == "f1" and h == 0 and have_attn)
                wm = (kind == "f1" and h == 0 and have_attn)
                res_mark, pbfree = emit_groups(kind, h, lnd_safe, single, with_mask=wm)
                flg = dve_flush()
                resfin = [res_mark[i + 1] if i + 1 < NCH else flg for i in range(NCH)]
            for ci in range(4):
                dstd = bass.AP(outT, 512 * h + 8 * ci * 128 * 2 * TH,
                               [[2 * TH, 128], [128 * 2 * TH, 8], [1, 512]])
                srcd = res[:, 512 * 8 * ci:512 * 8 * (ci + 1)].rearrange(
                    "p (f t) -> p f t", f=8)
                if ci < 3:
                    w = [res_mark[8 * ci + 8]]
                else:
                    w = [max(res_mark, key=lambda cv: cv[1]), dve_flush()]
                dma("sync", dstd, srcd, sSt, w)
            store_done = sSt.n
        if parts < 3:
            src = bass.AP(xT, 512, [[2 * TH, 128], [128 * 2 * TH, NCH], [1, 512]])
            dst = res[:, :].rearrange("p (f t) -> p f t", f=NCH)
            dma("sync", dst, src, sW, [(sSt, store_done)])
            dstd = bass.AP(outT, 512, [[2 * TH, 128], [128 * 2 * TH, NCH], [1, 512]])
            dma("sync", dstd, res[:, :].rearrange("p (f t) -> p f t", f=NCH), sSt,
                [(sW, sW.n)])
            store_done = sSt.n

        final_st = sSt.n
        E("sync", lambda eng: None, [(sSt, final_st)])

        block = ctx.enter_context(nc.Block())

        def replay(eng_name):
            def body(eng):
                for e, f in plan:
                    if e == eng_name:
                        f(eng)
            return body
        block.sync(replay("sync"))
        block.gpsimd(replay("gpsimd"))
        block.scalar(replay("scalar"))
        block.vector(replay("vector"))
        block.tensor(replay("tensor"))
    return nc


def _consts():
    a = np.arange(128)[:, None]
    j = np.arange(D)[None, :]
    RepN = (a == (j // 32)).astype(BF)
    mt = j // 128
    nb = j % 128
    y = np.arange(256)[None, :]
    RepT = (y == a // 32 + 124).astype(BF)
    ident = np.eye(128, dtype=np.float32).astype(BF)
    onesK = np.ones((128, 1), dtype=np.float32).astype(BF)
    ones1f = np.ones((1, 128), dtype=np.float32)
    return RepN, RepT, ident, onesK, ones1f


def kernel(L0_attn_data, L0_attn_mask, L0_attn_weight, L0_attn_bias, L0_ffn_w, _parts=6):
    X = np.ascontiguousarray(np.asarray(L0_attn_data, dtype=np.float32)).reshape(8192, D)
    mask = np.ascontiguousarray(np.asarray(L0_attn_mask, dtype=np.int32))
    waTh = np.ascontiguousarray(np.asarray(L0_attn_weight, dtype=np.float32).T)
    wfh = np.ascontiguousarray(np.asarray(L0_ffn_w, dtype=np.float32))
    bias_rh = np.ascontiguousarray(
        np.asarray(L0_attn_bias, dtype=np.float32).reshape(32, 128).T)
    RepN, RepT, ident, onesK, ones1f = _consts()

    nc = build_program(parts=_parts)
    in_maps = []
    for c in range(8):
        xTc = np.ascontiguousarray(X[1024 * c:1024 * (c + 1)].T)
        in_maps.append(dict(xT=xTc, maskd=mask, waT=waTh, wf=wfh, bias_r=bias_rh,
                            RepN=RepN, RepT=RepT, ident=ident, onesK=onesK,
                            ones1f=ones1f))
    res = run_bass_kernel_spmd(nc, in_maps, core_ids=list(range(8)))
    O = np.empty((8192, D), dtype=np.float32)
    for c in range(8):
        O[1024 * c:1024 * (c + 1)] = np.asarray(res.results[c]["outT"]).T
    return O.reshape(4, 2048, D)
